# version 01 (final)
# baseline (speedup 1.0000x reference)
import contextlib
import os
import numpy as np
import ml_dtypes
import concourse.bass as bass
import concourse.mybir as mybir
from concourse.bass_utils import run_bass_kernel_spmd

F32 = mybir.dt.float32
BF16 = mybir.dt.bfloat16
U8 = mybir.dt.uint8
AF = mybir.ActivationFunctionType
ALU = mybir.AluOpType
AX = mybir.AxisListType

T = 4096
NE = 8
PAD = 8
TC = T + PAD
HEADS = 8
DH = 64
WIN = 512
NWIN = T // WIN
NTT = T // 128
HID = 2730
HIDP = 2816
NKT2 = HIDP // 128
NOF = 44
DEPTH = 2
NQT = 4
ACONST = 30.0


def _rotary_tables():
    d = np.arange(0, DH, 2, dtype=np.float64)
    inv_freq = 1.0 / 10000.0 ** (d / DH)
    t = np.arange(T, dtype=np.float64)
    freqs = t[:, None] * inv_freq[None, :]
    base = (d + 0.4 * DH) / (1.4 * DH)
    power = (t - T // 2) / float(WIN // 2)
    scale = base[None, :] ** power[:, None]
    cos, sin = np.cos(freqs), np.sin(freqs)
    tabs = np.stack([cos * scale * 8.0, sin * scale * 8.0, cos / scale, sin / scale])
    tabs = tabs.astype(np.float32).reshape(4, NTT, 128, 32).astype(ml_dtypes.bfloat16)
    return np.ascontiguousarray(tabs)


def _tri_masks():
    i = np.arange(128)[:, None]
    j = np.arange(128)[None, :]
    m1 = np.where(j < i, np.float32(-1e9), np.float32(0.0))
    m2 = np.where(j > i, np.float32(-1e9), np.float32(0.0))
    return np.ascontiguousarray(np.stack([m1, m2]).astype(ml_dtypes.bfloat16))


def _band(w, qt):
    lo = max(qt, 4) if w == 0 else qt
    out = []
    for kt in range(lo, qt + 5):
        if kt == qt and w != 0:
            out.append((kt, 'hi'))
        elif kt == qt + 4:
            out.append((kt, 'lo'))
        else:
            out.append((kt, None))
    return out


def _kband(w, kt):
    qts = [qt for qt in range(NQT) if any(k2 == kt for (k2, _m) in _band(w, qt))]
    if not qts:
        return None
    q_lo, q_hi = min(qts) * 128, max(qts) * 128 + 128
    tris = []
    for qt in qts:
        for (k2, m) in _band(w, qt):
            if k2 == kt and m is not None:
                tris.append((qt, m))
    return q_lo, q_hi, tris


def _prep_shared(inputs):
    f32 = np.float32
    d = {}
    d["a_wqkv"] = np.ascontiguousarray(
        np.concatenate([np.asarray(inputs["L0_attn_Wq"], f32),
                        np.asarray(inputs["L0_attn_Wk"], f32),
                        np.asarray(inputs["L0_attn_Wv"], f32)], axis=1)
        .reshape(NE, 128, 192).transpose(1, 0, 2))
    d["f_wqkv"] = np.ascontiguousarray(np.asarray(inputs["L0_ffn_wqkv"], f32).reshape(DEPTH, NE, 128, 1536))
    d["f_wo"] = np.ascontiguousarray(
        np.asarray(inputs["L0_ffn_wo"], f32).reshape(DEPTH, HEADS, DH, 8, 128))
    w1 = np.asarray(inputs["L0_ffn_w1"], f32)
    a = np.zeros((DEPTH, 1024, HIDP), f32); a[:, :, :HID] = w1[:, :, :HID]
    g = np.zeros((DEPTH, 1024, HIDP), f32); g[:, :, :HID] = w1[:, :, HID:]
    a = a.reshape(DEPTH, NE, 128, NKT2, 128)
    g = g.reshape(DEPTH, NE, 128, NKT2, 128)
    w1i = np.stack([a, g], axis=3)
    d["f_w1"] = np.ascontiguousarray(
        w1i.transpose(0, 4, 3, 2, 1, 5).reshape(DEPTH, NOF, 128, NE, 128))
    b1 = np.asarray(inputs["L0_ffn_b1"], f32)
    ba = np.zeros((DEPTH, HIDP), f32); ba[:, :HID] = b1[:, :HID]
    bg = np.zeros((DEPTH, HIDP), f32); bg[:, :HID] = b1[:, HID:]
    b1i = np.stack([ba.reshape(DEPTH, NKT2, 128), bg.reshape(DEPTH, NKT2, 128)], axis=2)
    d["f_b1"] = np.ascontiguousarray(b1i.reshape(DEPTH, NOF, 128).transpose(2, 0, 1))
    w2 = np.asarray(inputs["L0_ffn_w2"], f32)
    w2p = np.zeros((DEPTH, HIDP, 1024), f32); w2p[:, :HID] = w2
    d["f_w2"] = np.ascontiguousarray(
        w2p.reshape(DEPTH, NKT2, 128, NE, 128).transpose(0, 3, 2, 1, 4))
    d["f_b2"] = np.ascontiguousarray(
        np.asarray(inputs["L0_ffn_b2"], f32).reshape(DEPTH, NE, 128).transpose(2, 0, 1))
    d["f_lng"] = np.ascontiguousarray(
        np.asarray(inputs["L0_ffn_ln_g"], f32).reshape(DEPTH, NE, 1, 128).astype(ml_dtypes.bfloat16))
    d["f_lnb"] = np.ascontiguousarray(
        np.asarray(inputs["L0_ffn_ln_b"], f32).reshape(DEPTH, NE, 128).transpose(2, 0, 1))
    qs = np.asarray(inputs["L0_ffn_q_scale"], f32)
    ks = np.asarray(inputs["L0_ffn_k_scale"], f32)
    d["f_qs"] = np.ascontiguousarray(np.broadcast_to(qs[:, None, :], (DEPTH, 128, DH)).copy())
    d["f_ks"] = np.ascontiguousarray(np.broadcast_to(ks[:, None, :], (DEPTH, 128, DH)).copy())
    d["c_rot"] = _rotary_tables()
    d["c_tri"] = _tri_masks()
    d["c_idb"] = np.ascontiguousarray(np.eye(128, dtype=np.float32).astype(ml_dtypes.bfloat16))
    d["c_idf"] = np.ascontiguousarray(np.eye(128, dtype=np.float32))
    return d


ENGS = ("te", "ve", "se", "sy", "gp")
QUEUES = ("sy", "gp")


PARANOID = os.environ.get("KERN_PARANOID", "0") == "1"
LAG = int(os.environ.get("KERN_LAG", "0"))


class Em:

    def __init__(self):
        self.ops = {e: [] for e in ENGS}
        self.cnt = {e: 0 for e in ENGS}
        self.marks = {e: {p: 0 for p in ENGS} for e in ENGS}
        self.maxw = {e: 0 for e in ENGS}

    def add(self, eng, fn, waits=(), pub=0):
        if PARANOID:
            waits = [(p, self.cnt[p]) for p in ENGS if p != eng or p in QUEUES]
            pub = pub or (16 if eng in QUEUES else 1)
        todo = []
        for ev in waits:
            if ev is None:
                continue
            peng, val = ev
            if peng == eng and peng not in QUEUES:
                continue
            if peng not in QUEUES:
                val = val + LAG
            if self.marks[eng][peng] >= val:
                continue
            assert val <= self.cnt[peng] + LAG, f"wait on future event {peng}:{val} (have {self.cnt[peng]})"
            self.marks[eng][peng] = val
            self.maxw[peng] = max(self.maxw[peng], val)
            todo.append((peng, val))
        if pub:
            self.cnt[eng] += pub
        ev_out = (eng, self.cnt[eng]) if pub else None

        def closure(h, sems, _todo=tuple(todo), _fn=fn, _pub=pub, _eng=eng):
            for (peng, val) in _todo:
                h.wait_ge(sems[peng], val)
            inst = _fn(h)
            if _pub:
                inst.then_inc(sems[_eng], _pub)
        self.ops[eng].append(closure)
        return ev_out

    def barrier(self):
        evs = []
        for e in ENGS:
            if e in QUEUES:
                evs.append(self.add(e, lambda h: h.nop(), waits=[(e, self.cnt[e])], pub=16))
            else:
                evs.append(self.add(e, lambda h: h.drain(), pub=1))
        for e in ENGS:
            self.add(e, lambda h: h.nop(), waits=evs, pub=(16 if e in QUEUES else 1))

    def check(self):
        for e in ENGS:
            while self.maxw[e] > self.cnt[e]:
                if e in QUEUES:
                    self.add(e, lambda h: h.nop(), waits=[(e, self.cnt[e])], pub=16)
                else:
                    self.add(e, lambda h: h.drain(), pub=1)


def build_program(stage="FULL"):
    nc = bass.Bass("TRN2", target_bir_lowering=False)
    em = Em()
    dbg = stage != "FULL"

    x_h = nc.dram_tensor("x", [NTT, 128, 1024], F32, kind="ExternalInput").ap()
    a_wqkv_h = nc.dram_tensor("a_wqkv", [128, NE, 192], F32, kind="ExternalInput").ap()
    f_wqkv_h = nc.dram_tensor("f_wqkv", [DEPTH, NE, 128, 1536], F32, kind="ExternalInput").ap()
    f_wo_h = nc.dram_tensor("f_wo", [DEPTH, HEADS, DH, 8, 128], F32, kind="ExternalInput").ap()
    f_w1_h = nc.dram_tensor("f_w1", [DEPTH, NOF, 128, NE, 128], F32, kind="ExternalInput").ap()
    f_b1_h = nc.dram_tensor("f_b1", [128, DEPTH, NOF], F32, kind="ExternalInput").ap()
    f_w2_h = nc.dram_tensor("f_w2", [DEPTH, NE, 128, NKT2, 128], F32, kind="ExternalInput").ap()
    f_b2_h = nc.dram_tensor("f_b2", [128, DEPTH, NE], F32, kind="ExternalInput").ap()
    f_lng_h = nc.dram_tensor("f_lng", [DEPTH, NE, 1, 128], BF16, kind="ExternalInput").ap()
    f_lnb_h = nc.dram_tensor("f_lnb", [128, DEPTH, NE], F32, kind="ExternalInput").ap()
    f_qs_h = nc.dram_tensor("f_qs", [DEPTH, 128, DH], F32, kind="ExternalInput").ap()
    f_ks_h = nc.dram_tensor("f_ks", [DEPTH, 128, DH], F32, kind="ExternalInput").ap()
    c_rot_h = nc.dram_tensor("c_rot", [4, NTT, 128, 32], BF16, kind="ExternalInput").ap()
    c_tri_h = nc.dram_tensor("c_tri", [2, 128, 128], BF16, kind="ExternalInput").ap()
    c_idb_h = nc.dram_tensor("c_idb", [128, 128], BF16, kind="ExternalInput").ap()
    c_idf_h = nc.dram_tensor("c_idf", [128, 128], F32, kind="ExternalInput").ap()

    out_h = nc.dram_tensor("out", [NTT, 128, 1024], F32, kind="ExternalOutput").ap()
    x1_h = nc.dram_tensor("x1spill", [NTT, 128, 1024], F32, kind="Internal").ap()
    w1b_h = nc.dram_tensor("w1b", [DEPTH, NOF, 128, NE, 128], BF16, kind="Internal").ap()
    w2b_h = nc.dram_tensor("w2b", [DEPTH, NE, 128, NKT2, 128], BF16, kind="Internal").ap()
    dbg_h = nc.dram_tensor("dbg", [128, NE, TC], BF16, kind="ExternalOutput").ap() if dbg else None

    es = contextlib.ExitStack()
    with es:
        es.enter_context(nc.allow_low_precision(
            reason="residual stream held in bf16 by design; fp32 stats/elementwise"))

        def sbuf(name, shape, dt):
            return es.enter_context(nc.sbuf_tensor(name, shape, dt))

        yT = sbuf("yT", [128, NE, TC], BF16)
        rot = sbuf("rot", [128, 4, NTT, 32], BF16)
        tri = sbuf("tri", [128, 2, 128], BF16)
        idb = sbuf("idb", [128, 128], BF16)
        idf = sbuf("idf", [128, 128], F32)
        onesb = sbuf("onesb", [128, 128], BF16)
        cc_sb = sbuf("cc_sb", [128, 8], F32)
        jv_sb = sbuf("jv_sb", [128, 8], F32)
        qs_sb = sbuf("qs_sb", [128, DEPTH, DH], F32)
        ks_sb = sbuf("ks_sb", [128, DEPTH, DH], F32)
        lnb_sb = sbuf("lnb_sb", [128, DEPTH, NE], F32)
        b1_sb = sbuf("b1_sb", [128, DEPTH, NOF], F32)
        b2_sb = sbuf("b2_sb", [128, DEPTH, NE], F32)
        lng_sb = sbuf("lng_sb", [1, DEPTH * NE * 128], BF16)
        wqkv_sb = sbuf("wqkv_sb", [128, NE, 1536], BF16)
        wo_sb = sbuf("wo_sb", [128, HEADS, 8, 128], BF16)
        w1w = sbuf("w1w", [128, 4, NE, 128], BF16)
        w2w = sbuf("w2w", [128, 2, NKT2, 128], BF16)

        AR_BYTES = 71296
        arena = sbuf("arena", [128, AR_BYTES], U8)

        def view(off, shape, dt):
            bb = {F32: 4, BF16: 2}[dt]
            n = int(np.prod(shape[1:])) * bb
            assert off + n <= AR_BYTES, f"view overflow {off + n}"
            ap = arena[:, off:off + n].bitcast(dt)
            if len(shape) > 2:
                names = [f"d{i}" for i in range(len(shape) - 1)]
                kw = {n_: s for n_, s in zip(names, shape[1:])}
                ap = ap.rearrange(f"p ({' '.join(names)}) -> p {' '.join(names)}", **kw)
            return ap

        class Ar:
            def __init__(self, base):
                self.off = base
            def get(self, shape, dt, align=64):
                bb = {F32: 4, BF16: 2}[dt]
                n = int(np.prod(shape[1:])) * bb
                self.off = (self.off + align - 1) // align * align
                o = self.off
                self.off += n
                assert self.off <= AR_BYTES, f"arena overflow {self.off}"
                return view(o, shape, dt)

        P = [es.enter_context(nc.psum_tensor(f"ps{i}", [128, 512], F32)) for i in range(8)]
        Pb = [p[:, :].bitcast(BF16) for p in P]
        sems = {e: es.enter_context(nc.semaphore(f"sem_{e}")) for e in ENGS}
        blk = es.enter_context(nc.Block())

        def MM(out, lhsT, rhs, st_, sp, tp=None):
            return lambda h, a=out, b=lhsT, c=rhs, s=st_, t=sp, u=tp: h.matmul(
                out=a, lhsT=b, rhs=c, start=s, stop=t, tile_position=u)

        def TRB(out, in_):
            return lambda h, a=out, b=in_: h.transpose(out=a, in_=b, identity=idb[:, :])

        def TRF(out, in_, ident):
            return lambda h, a=out, b=in_, c=ident: h.transpose(out=a, in_=b, identity=c)

        def ACT(out, in_, f, bias=0.0, scale=1.0, acc=None):
            return lambda h, a=out, b=in_, ff=f, bi=bias, sc=scale, ac=acc: h.activation(
                out=a, in_=b, func=ff, bias=bi, scale=sc, accum_out=ac)

        def SCP(out, in_):
            return lambda h, a=out, b=in_: h.copy(out=a, in_=b)

        def VCP(out, in_):
            return lambda h, a=out, b=in_: h.tensor_copy(out=a, in_=b)

        def TT(out, a, b, op):
            return lambda h, o=out, x=a, y=b, z=op: h.tensor_tensor(out=o, in0=x, in1=y, op=z)

        def TS(out, in0, s1, s2, o1, o2):
            return lambda h, a=out, b=in0, c=s1, d=s2, e=o1, f=o2: h.tensor_scalar(
                out=a, in0=b, scalar1=c, scalar2=d, op0=e, op1=f)

        def TS1(out, in0, s, op):
            return lambda h, a=out, b=in0, c=s, d=op: h.tensor_single_scalar(out=a, in_=b, scalar=c, op=d)

        def STT(out, in0, s, in1, o0, o1):
            return lambda h, a=out, b=in0, c=s, d=in1, e=o0, f=o1: h.scalar_tensor_tensor(
                out=a, in0=b, scalar=c, in1=d, op0=e, op1=f)

        def RED(out, in_, op, neg=False):
            return lambda h, a=out, b=in_, c=op, d=neg: h.tensor_reduce(
                out=a, in_=b, axis=AX.X, op=c, negate=d)

        def RCP(out, in_):
            return lambda h, a=out, b=in_: h.reciprocal(out=a, in_=b)

        def MSET(ap, v):
            return lambda h, a=ap, b=v: h.memset(a, b)

        def DMA(out, in_):
            return lambda h, a=out, b=in_: h.dma_start(out=a, in_=b)

        def te(fn, w=(), p=0):
            return em.add("te", fn, w, p)

        def vpad(w=()):
            em.add("ve", MSET(jv_sb[:, 0:1], 0.0), w)
            em.add("ve", MSET(jv_sb[:, 1:2], 0.0))
            return em.add("ve", MSET(jv_sb[:, 2:3], 0.0), pub=1)

        def spad(w=()):
            em.add("se", SCP(jv_sb[:, 3:4], cc_sb[:, 3:4]), w)
            em.add("se", SCP(jv_sb[:, 3:4], cc_sb[:, 3:4]))
            return em.add("se", SCP(jv_sb[:, 3:4], cc_sb[:, 3:4]), pub=1)

        def ve(fn, w=(), p=0):
            return em.add("ve", fn, w, p)

        def se(fn, w=(), p=0):
            return em.add("se", fn, w, p)

        def sy(fn, w=(), p=0):
            return em.add("sy", fn, w, p)

        def gp(fn, w=(), p=0):
            return em.add("gp", fn, w, p)

        for dst, src in [
            (rot[:, :, :, :], c_rot_h.rearrange("f t p d -> p f t d")),
            (tri[:, :, :], c_tri_h.rearrange("m p c -> p m c")),
            (idb[:, :], c_idb_h),
            (idf[:, :], c_idf_h),
            (qs_sb[:, :, :], f_qs_h.rearrange("l p d -> p l d")),
            (ks_sb[:, :, :], f_ks_h.rearrange("l p d -> p l d")),
            (lnb_sb[:, :, :], f_lnb_h),
            (b1_sb[:, :, :], f_b1_h),
            (b2_sb[:, :, :], f_b2_h),
            (lng_sb[:, :], f_lng_h.rearrange("l e o c -> o (l e c)")),
        ]:
            sy(DMA(dst, src), p=16)
        ev_const = ("sy", em.cnt["sy"])
        ve(MSET(onesb[:, :], 1.0), p=1)
        ve(MSET(cc_sb[:, 0:1], -ACONST), p=1)
        ve(MSET(cc_sb[:, 1:2], 1e-5), p=1)
        ve(MSET(cc_sb[:, 2:3], 1e-30), p=1)
        ve(MSET(cc_sb[:, 3:4], 1.0 / 1024), p=1)
        ev_ones = ve(MSET(yT[:, :, 0:PAD], 0.0), p=1)

        ptil_dbg = [None] * 5
        Aw = Ar(0)
        wstg = Aw.get([128, 1, 2112], F32)
        BASE = Aw.off

        A = Ar(BASE)
        outT = A.get([128, T], F32)
        TAIL_BASE = A.off
        awb = A.get([128, NE, 192], BF16)
        xTc = A.get([128, 1, NE, 512], BF16)
        xf = A.get([128, 2, 1024], F32)
        xbf = A.get([128, 1, 1024], BF16)
        qT = A.get([128, T], BF16)
        kT = A.get([128, T], BF16)
        v_a = A.get([128, NTT, DH], BF16)
        pexp = A.get([128, 2, 512], BF16)
        zbuf = A.get([128, 16], F32)
        zdump = A.get([128, NTT], F32)

        ring = {"in": [None, None], "out": [None, None], "i": 0}
        pending = []

        def prep_job(src, dst, n, to_dram, rows=128):
            pending.append((src, dst, n, to_dram, rows))

        def pump(k):
            for _ in range(k):
                if not pending:
                    return
                src, dst, n, to_dram, rows = pending.pop(0)
                s = 0
                ring["i"] += 1
                evi = sy(DMA(wstg[0:rows, s, 0:n], src), w=[ring["out"][s]], p=16)
                if to_dram:
                    assert n <= 1408
                    bfv = wstg[0:rows, s, 1408:1408 + (n + 1) // 2].bitcast(BF16)[0:rows, 0:n]
                    ve(VCP(bfv, wstg[0:rows, s, 0:n]), w=[evi], p=1)
                    evc = vpad()
                    ring["out"][s] = gp(DMA(dst, bfv), w=[evc], p=16)
                else:
                    ring["out"][s] = ve(VCP(dst, wstg[0:rows, s, 0:n]), w=[evi], p=1)
                last_prep[0] = ring["out"][s]

        last_prep = [None]

        def load_wqkv_wo(l):
            evs = []
            for e in range(NE):
                prep_job(f_wqkv_h[l, e], wqkv_sb[:, e, :], 1536, False)
            for hh in range(HEADS):
                prep_job(f_wo_h[l, hh].rearrange("p o c -> p (o c)"),
                         wo_sb[0:DH, hh, :, :].rearrange("p o c -> p (o c)"), 1024, False, rows=DH)

        def queue_tail(l):
            for of in range(NOF):
                for half in range(2):
                    prep_job(f_w1_h[l, of, :, 4 * half:4 * half + 4, :].rearrange("p e a -> p (e a)"),
                             w1b_h[l, of, :, 4 * half:4 * half + 4, :].rearrange("p e a -> p (e a)"),
                             4 * 128, True)
            for ot in range(NE):
                for half in range(2):
                    prep_job(f_w2_h[l, ot, :, 11 * half:11 * half + 11, :].rearrange("p k a -> p (k a)"),
                             w2b_h[l, ot, :, 11 * half:11 * half + 11, :].rearrange("p k a -> p (k a)"),
                             11 * 128, True)

        prep_job(a_wqkv_h.rearrange("p e c -> p (e c)"),
                 awb[:, :, :].rearrange("p e c -> p (e c)"), NE * 192, False)
        load_wqkv_wo(0)
        queue_tail(0)
        pump(1 + NE + HEADS)
        ev_w0 = last_prep[0]

        ev_xbf = [None, None]
        ev_evac = None
        ev_vprev = None
        ev_qk = [None, None]
        ev_qkevac = None

        for qc in range(8):
            s2 = 0
            for i in range(4):
                tt = qc * 4 + i
                s = 0
                evx = gp(DMA(xf[:, s, :], x_h[tt]), w=[ev_xbf[s]], p=16)
                ev_xbf[s] = ve(VCP(xbf[:, s, :], xf[:, s, :]), w=[evx], p=1)
                for e in range(NE):
                    evt = te(TRB(Pb[7][0:128, e * 128:(e + 1) * 128], xbf[:, s, e * 128:(e + 1) * 128]),
                             w=([ev_xbf[s], ev_evac, ev_const] if e == 0 else ()),
                             p=(1 if e == NE - 1 else 0))
                ev_evac = ve(VCP(xTc[:, s2, :, i * 128:(i + 1) * 128],
                                 Pb[7][:, 0:1024].rearrange("p (e c) -> p e c", e=8)),
                             w=[evt, ev_qk[s2]], p=1)
                for e in range(NE):
                    evv = te(MM(P[5][:, 0:64], xTc[:, s2, e, i * 128:(i + 1) * 128], awb[:, e, 128:192],
                                e == 0, e == NE - 1),
                             w=([ev_evac, ev_w0, ev_vprev] if e == 0 else ()),
                             p=(1 if e == NE - 1 else 0))
                ev_vprev = ve(VCP(v_a[:, tt, :], P[5][:, 0:64]), w=[evv], p=1)
                pump(5)
            for e in range(NE):
                evq = te(MM(P[3][0:64, 0:512], awb[:, e, 0:64], xTc[:, s2, e, :], e == 0, e == NE - 1),
                         w=([ev_qkevac] if e == 0 else ()), p=(1 if e == NE - 1 else 0))
            for e in range(NE):
                evk = te(MM(P[4][0:64, 0:512], awb[:, e, 64:128], xTc[:, s2, e, :], e == 0, e == NE - 1),
                         p=(1 if e == NE - 1 else 0))
            ev_qk[s2] = evk
            ve(VCP(qT[0:64, qc * 512:(qc + 1) * 512], P[3][0:64, 0:512]), w=[evq], p=1)
            ev_qkevac = ve(VCP(kT[0:64, qc * 512:(qc + 1) * 512], P[4][0:64, 0:512]), w=[evk], p=1)
        em.barrier()

        if stage == "QKV":
            ev1 = em.add("gp", lambda h: h.dma_start(out=dbg_h[0:64, 0, 0:T], in_=qT[0:64, :]), pub=16)
            ev2 = em.add("gp", lambda h: h.dma_start(out=dbg_h[0:64, 1, 0:T], in_=kT[0:64, :]), pub=16)
            ev3 = em.add("gp", lambda h: h.dma_start(
                out=dbg_h[:, 2, 0:NTT * DH], in_=v_a[:, :, :].rearrange("p t d -> p (t d)")), pub=16)
            em.add("gp", lambda h: h.nop(), waits=[ev1, ev2, ev3])
            em.barrier()
            em.check()
            _emit(em, sems, blk)
            return nc

        pexp2 = view(BASE, [128, 6, 512], BF16)
        def pch(qc):
            return pexp[:, qc, :] if qc < 2 else pexp2[:, qc - 2, :]
        ev_exp = [None, None]
        ev_red = None
        NKT1 = 3 if stage == "AT0" else NTT
        for kt in range(NKT1):
            evz = None
            for qc in range(8):
                s = qc % 2
                evm = te(MM(P[s][:, :], kT[0:64, kt * 128:(kt + 1) * 128],
                            qT[0:64, qc * 512:(qc + 1) * 512], True, True), w=[ev_exp[s]], p=1)
                ev_exp[s] = se(ACT(pch(qc), P[s][:, :], AF.Exp, bias=cc_sb[:, 0:1]),
                               w=[evm, ev_red], p=1)
                evz = ev_exp[s]
            ve(RED(zbuf[:, 10:11], pexp[:, :, :].rearrange("p a b -> p (a b)"), ALU.add), w=[evz], p=1)
            ve(RED(zbuf[:, 11:12], pexp2[:, :, :].rearrange("p a b -> p (a b)"), ALU.add), p=1)
            vpad()
            ve(TT(zbuf[:, 8:9], zbuf[:, 10:11], zbuf[:, 11:12], ALU.add), p=1)
            vpad()
            ve(RCP(zbuf[:, 9:10], zbuf[:, 8:9]), p=1)
            vpad()
            ev_red = ve(TS1(v_a[:, kt, :], v_a[:, kt, :], zbuf[:, 9:10], ALU.mult), p=1)
            pump(2)
        em.barrier()

        if stage == "AT0":
            ev3 = em.add("gp", lambda h: h.dma_start(
                out=dbg_h[:, 0, 0:1024], in_=pexp[:, :, :].rearrange("p a b -> p (a b)")), pub=16)
            ev4 = em.add("gp", lambda h: h.dma_start(
                out=dbg_h[:, 1, 0:3072], in_=pexp2[:, :, :].rearrange("p a b -> p (a b)")), pub=16)
            ev5 = em.add("gp", lambda h: h.dma_start(
                out=out_h[0, :, 0:NTT], in_=zdump[:, :]), pub=16)
            em.add("gp", lambda h: h.nop(), waits=[ev3, ev4, ev5])
            em.barrier()
            em.check()
            _emit(em, sems, blk)
            return nc

        if stage == "AT1":
            ev3 = em.add("gp", lambda h: h.dma_start(
                out=dbg_h[:, 2, 0:NTT * DH], in_=v_a[:, :, :].rearrange("p t d -> p (t d)")), pub=16)
            ev4 = em.add("gp", lambda h: h.dma_start(
                out=out_h[0, :, 0:NTT], in_=zdump[:, :]), pub=16)
            em.add("gp", lambda h: h.nop(), waits=[ev3, ev4])
            em.barrier()
            em.check()
            _emit(em, sems, blk)
            return nc

        ev_exp = [None, None]
        ev_pv = None
        ev_outTe = None
        for qc in range(8):
            for kt in range(NTT):
                s = (qc * NTT + kt) % 2
                evm = te(MM(P[s][:, :], kT[0:64, kt * 128:(kt + 1) * 128],
                            qT[0:64, qc * 512:(qc + 1) * 512], True, True),
                         w=[ev_exp[s]], p=1)
                ev_exp[s] = se(ACT(pexp[:, s, :], P[s][:, :], AF.Exp, bias=cc_sb[:, 0:1]), w=[evm], p=1)
                ev_pv = te(MM(P[2][0:64, :], v_a[:, kt, :], pexp[:, s, :], kt == 0, kt == NTT - 1),
                           w=[ev_exp[s]] + ([ev_outTe] if kt == 0 else []), p=1)
            ev_outTe = ve(VCP(outT[0:64, qc * 512:(qc + 1) * 512], P[2][0:64, :]), w=[ev_pv], p=1)
            pump(4)
        em.barrier()

        if stage == "AT2":
            evs = []
            for i in range(4):
                evs.append(em.add("gp", lambda h, i=i: h.dma_start(
                    out=out_h[i, 0:64, :], in_=outT[0:64, i * 1024:(i + 1) * 1024]), pub=16))
            em.add("gp", lambda h: h.nop(), waits=evs)
            em.barrier()
            em.check()
            _emit(em, sems, blk)
            return nc

        At = Ar(TAIL_BASE)
        out_tok = At.get([128, NTT, DH], F32)
        sumb = At.get([128, 2, 1024], F32)
        sqb_a = At.get([128, 1024], F32)
        y1b_ = At.get([128, 2, 1024], F32)
        y1c = At.get([128, 2, 1024], BF16)
        statc = At.get([128, 16], F32)

        ev_ot = [None] * 4
        ev_sum = [None, None]
        ev_y1 = [None, None]
        ev_y1g = [None, None]
        ev_y1c = [None, None]
        ev_y1tr = [None, None]
        ev_xfA = [None, None]
        ev_yTev = None
        ev_sq = None
        for tt in range(NTT):
            q4 = tt % 4
            evt = te(TRF(P[6][:, q4 * 128:q4 * 128 + 64], outT[0:64, tt * 128:(tt + 1) * 128],
                         idf[0:64, 0:64]), w=[ev_ot[q4]], p=1)
            evo = ve(VCP(out_tok[:, tt, :], P[6][:, q4 * 128:q4 * 128 + 64]), w=[evt], p=1)
            ev_ot[q4] = evo
            s = tt % 2
            ev_xfA[s] = gp(DMA(xf[:, s, :], x_h[tt]), w=[ev_sum[s], ev_xfA[s]], p=16)
            ev_sum[s] = ve(TT(sumb[:, s, :].rearrange("p (a b) -> p a b", a=16),
                              xf[:, s, :].rearrange("p (a b) -> p a b", a=16),
                              out_tok[:, tt:tt + 1, :].broadcast_to([128, 16, DH]),
                              ALU.add), w=[ev_xfA[s], evo], p=1)
            sm = sumb[:, s, :]
            ve(RED(statc[:, 0:1], sm, ALU.add), p=1)
            ve(TS1(statc[:, 1:2], statc[:, 0:1], cc_sb[:, 3:4], ALU.mult), p=1)
            evsq = se(ACT(sqb_a[:, :], sm, AF.Square), w=[ev_sum[s], ev_sq], p=1)
            ev_sq = ve(RED(statc[:, 2:3], sqb_a[:, :], ALU.add), w=[evsq], p=1)
            ve(TT(statc[:, 3:4], statc[:, 1:2], statc[:, 1:2], ALU.mult), p=1)
            evv = ve(STT(statc[:, 4:5], statc[:, 2:3], cc_sb[:, 3:4], statc[:, 3:4],
                         ALU.mult, ALU.subtract), p=1)
            evsd = se(ACT(statc[:, 5:6], statc[:, 4:5], AF.Sqrt, bias=cc_sb[:, 1:2]), w=[evv], p=1)
            ve(RCP(statc[:, 6:7], statc[:, 5:6]), w=[evsd], p=1)
            ev_y1[s] = ve(TS(y1b_[:, s, :], sm, statc[:, 1:2], statc[:, 6:7],
                             ALU.subtract, ALU.mult), w=[ev_y1g[s]], p=1)
            ev_y1g[s] = gp(DMA(x1_h[tt], y1b_[:, s, :]), w=[ev_y1[s]], p=16)
            if stage == "A":
                ev_y1g[s] = gp(DMA(out_h[tt], y1b_[:, s, :]), w=[ev_y1[s]], p=16)
            ev_y1c[s] = ve(VCP(y1c[:, s, :], y1b_[:, s, :]), w=[ev_y1tr[s]], p=1)
            for e in range(NE):
                evt2 = te(TRB(Pb[7][0:128, e * 128:(e + 1) * 128], y1c[:, s, e * 128:(e + 1) * 128]),
                          w=([ev_y1c[s], ev_yTev] if e == 0 else ()), p=(1 if e == NE - 1 else 0))
            ev_y1tr[s] = evt2
            ev_yTev = ve(VCP(yT[:, :, PAD + tt * 128:PAD + (tt + 1) * 128],
                             Pb[7][:, 0:1024].rearrange("p (e c) -> p e c", e=8)),
                         w=[evt2], p=1)
            pump(2)
        em.barrier()

        if stage == "A":
            _dbg_dump(em, gp, DMA, dbg_h, yT)
            _emit(em, sems, blk)
            return nc

        def part1(l, wq_ev):
            B = Ar(BASE)
            lqk = B.get([128, 512], BF16)
            hbuf = B.get([128, 1, 512], BF16)
            sq2 = B.get([128, 512], F32)
            rtmp = sq2[:, 0:128].bitcast(BF16)
            rs = B.get([128, 16], F32)
            qaT = B.get([128, HEADS, WIN], BF16)
            kaT = B.get([128, HEADS, 2, WIN], BF16)
            v_f = B.get([128, 8, HEADS * 65], BF16)
            ptil = B.get([128, 8, WIN], BF16)
            delta = B.get([128, NE, WIN], BF16)
            nmc = B.get([128, 8], F32)
            nmcb = B.get([128, 8], BF16)
            zb2 = B.get([128, 512], BF16)
            qdbg = B.get([128, 512], BF16) if stage == "P1" else None
            oTh = B.get([128, HEADS, WIN], BF16)

            st = {'q': [], 'k': [], 'v': [], 'tr': None, 'tr_te': None, 'vf': {},
                  'attn_te': None, 'pv': None, 'upd': None, 'oT': None,
                  'zbc': None, 'p6war': None, 'expB': None, 'sq2w': None, 'rotw': None,
                  'rtw': None}
            ve(MSET(kaT[64:65, :, :, :], 1.0), p=1)
            ve(MSET(v_f[:, :, :].rearrange("p s (h u) -> p s h u", h=HEADS)[:, :, :, 64:65], 1.0), p=1)
            st['ptil_init'] = ve(MSET(ptil[:, :, :], 0.0), p=1)

            def qkv_tile(tt):
                evg = {}
                for grp, bank, c0, c1 in ((0, P[0], 0, 512), (1, P[1], 512, 1024), (2, P[2], 1024, 1536)):
                    warl = {0: st['q'], 1: st['k'], 2: st['v']}[grp]
                    for e in range(NE):
                        cc = PAD + tt * 128 - (1 if e >= 4 else 0)
                        evm = te(MM(bank[:, :], yT[:, e, cc:cc + 128], wqkv_sb[:, e, c0:c1],
                                    e == 0, e == NE - 1),
                                 w=(list(warl) + [wq_ev] if e == 0 else ()),
                                 p=(1 if e == NE - 1 else 0))
                    evg[grp] = evm
                slot = tt % 8
                evvf = ve(VCP(v_f[:, slot, :].rearrange("p (h u) -> p h u", h=HEADS)[:, :, 0:64],
                              P[2][:, :].rearrange("p (h u) -> p h u", h=HEADS)),
                          w=[evg[2], st['pv']], p=1)
                st['vf'][slot] = evvf
                st['v'] = [evvf]
                par = (tt // 4) % 2
                for side in range(2):
                    bank = (P[0], P[1])[side]
                    evmm = evg[side]
                    dst_l = lqk
                    dst_h = hbuf
                    sc_sb = (qs_sb, ks_sb)[side]
                    rcs = (0, 8)[side]
                    t1 = rot[:, 2 * side + 0, tt, :].unsqueeze(1).broadcast_to([128, 8, 32])
                    t2 = rot[:, 2 * side + 1, tt, :].unsqueeze(1).broadcast_to([128, 8, 32])
                    if tt == 31 and side == 1 and stage == "P1":
                        ve(VCP(qdbg[:, :], bank[:, :]), w=[evmm], p=1)
                    evs = se(ACT(sq2[:, :], bank[:, :], AF.Square), w=[evmm, st['sq2w'], st['rtw']], p=1)
                    ve(RED(rs[:, rcs:rcs + 8], sq2[:, :].rearrange("p (h d) -> p h d", d=DH),
                           ALU.add), w=[evs], p=1)
                    evr = vpad()
                    st['sq2w'] = evr
                    se(ACT(rs[:, rcs:rcs + 8], rs[:, rcs:rcs + 8], AF.Sqrt, bias=cc_sb[:, 2:3]), w=[evr], p=1)
                    evr = spad()
                    ve(RCP(rs[:, rcs:rcs + 8], rs[:, rcs:rcs + 8]), w=[evr], p=1)
                    evr = vpad()
                    l3 = dst_l[:, :].rearrange("p (h d) -> p h d", d=DH)
                    evl = ve(TT(l3, bank[:, :].rearrange("p (h d) -> p h d", d=DH),
                                rs[:, rcs:rcs + 8].unsqueeze(2).broadcast_to([128, 8, DH]),
                                ALU.mult), w=[evr, st['rotw']], p=1)
                    st[('q', 'k')[side]] = [evs, evl]
                    evl = ve(TT(l3, l3, sc_sb[:, l, :].unsqueeze(1).broadcast_to([128, 8, DH]),
                                ALU.mult), p=1)
                    h3 = dst_h[:, 0, :].rearrange("p (h d) -> p h d", d=DH)
                    r3 = rtmp[:, :].rearrange("p (h d) -> p h d", d=32)
                    ve(TT(h3[:, :, 0:32], l3[:, :, 0:32], t1, ALU.mult), w=[evl, st['rotw']], p=1)
                    ve(TT(r3, l3[:, :, 32:64], t2, ALU.mult), p=1)
                    ve(TT(h3[:, :, 0:32], h3[:, :, 0:32], r3, ALU.subtract), p=1)
                    ve(TT(h3[:, :, 32:64], l3[:, :, 32:64], t1, ALU.mult), p=1)
                    ve(TT(r3, l3[:, :, 0:32], t2, ALU.mult), p=1)
                    evrot = ve(TT(h3[:, :, 32:64], h3[:, :, 32:64], r3, ALU.add), p=1)
                    st['rtw'] = evrot
                    for hh in range(HEADS):
                        evt = te(TRB(Pb[7][0:64, hh * 128:(hh + 1) * 128],
                                     dst_h[:, 0, hh * 64:(hh + 1) * 64]),
                                 w=([evrot, st['tr']] if hh == 0 else ()), p=(1 if hh == HEADS - 1 else 0))
                    st['tr_te'] = evt
                    if side == 0:
                        dst = qaT[0:64, 0:HEADS, (tt % 4) * 128:(tt % 4) * 128 + 128]
                    else:
                        dst = kaT[0:64, 0:HEADS, par, (tt % 4) * 128:(tt % 4) * 128 + 128]
                    st['tr'] = ve(VCP(dst, Pb[7][0:64, 0:1024].rearrange("p (h c) -> p h c", h=HEADS)),
                                  w=[evt, st['attn_te']], p=1)
                    st['rotw'] = st['tr_te']
                return evg[2]

            def attn(w):
                par_prev, par_cur = (w - 1) % 2, w % 2
                ev_qready = st['tr']
                last_te = None
                for hh in range(HEADS):
                    ev_red = None
                    has_a = []
                    for qt in range(NQT):
                        blocks = _band(w, qt)
                        aw_ = [kt for kt, _m in blocks if kt < 4]
                        bw_ = [kt for kt, _m in blocks if kt >= 4]
                        has_a.append(bool(aw_))
                        evA = None
                        if aw_:
                            te(MM(P[3][:, 0:len(aw_) * 128],
                                  qaT[0:64, hh, qt * 128:(qt + 1) * 128],
                                  kaT[0:64, hh, par_prev, aw_[0] * 128:(aw_[-1] + 1) * 128],
                                  True, False),
                               w=[ev_qready, ev_red, st['expB']], p=0)
                            evA = te(MM(P[3][:, 0:128], idb[:, :], tri[:, 0, :], False, True), p=1)
                        te(MM(P[4][:, 0:len(bw_) * 128],
                              qaT[0:64, hh, qt * 128:(qt + 1) * 128],
                              kaT[0:64, hh, par_cur, (bw_[0] - 4) * 128:(bw_[-1] - 3) * 128],
                              True, False),
                           w=[ev_qready, ev_red, st['expB']], p=0)
                        evB = te(MM(P[4][:, (qt + 4 - bw_[0]) * 128:(qt + 5 - bw_[0]) * 128],
                                    idb[:, :], tri[:, 1, :], False, True), p=1)
                        if aw_:
                            ve(RED(nmc[:, qt:qt + 1], P[3][:, 0:len(aw_) * 128], ALU.max, True),
                               w=[evA], p=1)
                        ev_red = ve(RED(nmc[:, 4 + qt:5 + qt], P[4][:, 0:len(bw_) * 128], ALU.max, True),
                                    w=[evB], p=1)
                    vpad()
                    if all(has_a):
                        ve(TT(nmc[:, 0:4], nmc[:, 0:4], nmc[:, 4:8], ALU.min), p=1)
                        vpad()
                        evcb = ve(VCP(nmcb[:, 0:4], nmc[:, 0:4]), p=1)
                    else:
                        evcb = ve(VCP(nmcb[:, 0:4], nmc[:, 4:8]), p=1)
                    evcb = vpad()
                    for qt in range(NQT):
                        evmt = te(MM(P[6][64:65, qt * 128:(qt + 1) * 128], nmcb[:, qt:qt + 1],
                                     idb[:, :], True, True, tp=(0, 64)),
                                  w=([evcb, st['p6war']] if qt == 0 else ()), p=(1 if qt == NQT - 1 else 0))
                    ev_mrow = ve(VCP(qaT[64:65, hh, 0:512], P[6][64:65, 0:512]), w=[evmt], p=1)
                    ev_mrow = vpad()
                    kts = [kt for kt in range(8) if _kband(w, kt) is not None]
                    ev_exp_k = {}
                    for idx, kt in enumerate(kts):
                        q_lo, q_hi, tris = _kband(w, kt)
                        bank = P[3 + idx % 2]
                        par = par_prev if kt < 4 else par_cur
                        col = (kt % 4) * 128
                        evm = te(MM(bank[:, 0:q_hi - q_lo], kaT[0:65, hh, par, col:col + 128],
                                    qaT[0:65, hh, q_lo:q_hi], True, len(tris) == 0),
                                 w=[ev_mrow, ev_exp_k.get(kts[idx - 2]) if idx >= 2 else None],
                                 p=(1 if not tris else 0))
                        for i_, (qt, m) in enumerate(tris):
                            evm = te(MM(bank[:, qt * 128 - q_lo:qt * 128 - q_lo + 128], idb[:, :],
                                        tri[:, 1 if m == 'hi' else 0, :], False, i_ == len(tris) - 1),
                                     p=(1 if i_ == len(tris) - 1 else 0))
                        evex = se(ACT(ptil[:, kt, q_lo:q_hi], bank[:, 0:q_hi - q_lo], AF.Exp),
                                  w=[evm, st['pv'], st['ptil_init']], p=1)
                        ev_exp_k[kt] = evex
                    st['expB'] = ev_exp_k[kts[-1]]
                    evpv = None
                    for i_, kt in enumerate(kts):
                        g = 4 * (w - 1) + kt
                        slot = g % 8
                        evpv = te(MM(P[5][0:65, :], v_f[:, slot, hh * 65:(hh + 1) * 65],
                                     ptil[:, kt, 0:512], i_ == 0, i_ == len(kts) - 1),
                                  w=[ev_exp_k[kt], st['vf'].get(slot), st['oT']], p=1)
                    last_te = evpv
                    st['pv'] = evpv
                    ve(RCP(zb2[64:65, 0:512], P[5][64:65, 0:512]), w=[evpv], p=1)
                    evrz = vpad()
                    evbc = te(MM(P[6][:, 0:512], onesb[64:65, :], zb2[64:65, 0:512], True, True),
                              w=[evrz], p=1)
                    evzc = se(SCP(zb2[0:64, :], P[6][0:64, 0:512]), w=[evbc, st['zbc']], p=1)
                    st['p6war'] = evzc
                    st['oT'] = ve(TT(oTh[0:64, hh, :], P[5][0:64, :], zb2[0:64, :], ALU.mult),
                                  w=[evzc], p=1)
                    st['zbc'] = st['oT']
                st['attn_te'] = last_te
                return st['oT']

            def wo_update(w, ev_oT, ev_qkv_next):
                ev_d = []
                for ot in range(NE):
                    bank = P[3 + ot % 2]
                    for hh in range(HEADS):
                        evw = te(MM(bank[:, :], wo_sb[0:DH, hh, ot, :], oTh[0:DH, hh, :],
                                    hh == 0, hh == HEADS - 1),
                                 w=([ev_oT, wq_ev] + ([ev_d[ot - 2]] if ot >= 2 else [])
                                   if hh == 0 else ()),
                                 p=(1 if hh == HEADS - 1 else 0))
                    ev_d.append(se(SCP(delta[:, ot, :], bank[:, :]), w=[evw, st['upd']], p=1))
                evu = None
                for ot in range(NE):
                    evu = ve(TT(yT[:, ot, PAD + w * 512:PAD + (w + 1) * 512],
                                yT[:, ot, PAD + w * 512:PAD + (w + 1) * 512],
                                delta[:, ot, :], ALU.add),
                             w=[ev_d[ot], ev_qkv_next], p=1)
                st['upd'] = evu
                return evu

            ptil_dbg[0] = ptil
            ptil_dbg[1] = qaT
            ptil_dbg[2] = kaT
            ptil_dbg[3] = lqk
            ptil_dbg[4] = hbuf
            ptil_dbg.append(qdbg)
            for tt in range(4):
                ev_qkv = qkv_tile(tt)
                pump(4)
            for w in range(NWIN):
                ev_oT = attn(w)
                if w + 1 < NWIN:
                    for tt in range(4 * (w + 1), 4 * (w + 2)):
                        ev_qkv = qkv_tile(tt)
                        pump(4)
                wo_update(w, ev_oT, ev_qkv if w + 1 < NWIN else None)
            em.barrier()
            return oTh

        def part2(l):
            S0C = int(os.environ.get("S0C", "8"))
            S0P = int(os.environ.get("S0P", "3"))
            C = Ar(BASE)
            sqb = C.get([128, NE, 512], BF16)
            h2T = C.get([128, 1, NE, 512], BF16)
            h3 = C.get([128, NKT2, 512], BF16)
            gel = C.get([128, 2, 512], BF16)
            delta2 = C.get([128, NE, 512], BF16)
            statf = C.get([128, 2048], F32)
            statb = C.get([128, 1024], BF16)

            ev_h3 = [None] * NKT2
            ev_d2 = [None] * NE
            ev_upd = None
            ev_stats_te = None
            ev_dma1 = [None] * NOF
            ev_w1slot = [None] * 4
            ev_dma2 = [None, None]
            ev_w2war = [None, None]
            ev_gelw = [None, None]
            ev_row = None
            ev_apply = [None, None]
            ev_h2last = [None]
            ev_p4war = [None]
            ev_p5war = [None]
            nl1 = [0]

            def load_w1(of):
                ev_dma1[of] = sy(DMA(w1w[:, of % 4, :, :].rearrange("p e c -> p (e c)"),
                                     w1b_h[l, of].rearrange("p e c -> p (e c)")),
                                 w=[ev_w1slot[of % 4], last_prep[0]], p=16)

            def load_w2(ot):
                ev_dma2[ot % 2] = sy(DMA(w2w[:, ot % 2, :, :].rearrange("p k c -> p (k c)"),
                                         w2b_h[l, ot].rearrange("p k c -> p (k c)")),
                                     w=[ev_w2war[ot % 2], last_prep[0]], p=16)

            def shifted(e, c0, n=512):
                off = c0 - (1 if e >= 4 else 0)
                return yT[:, e, off:off + n]

            for c in range(S0C):
                c0 = PAD + c * 512
                cs = 0
                ev_sq = []
                for e in range(NE):
                    ev_sq.append(se(ACT(sqb[:, e, :], shifted(e, c0), AF.Square),
                                    w=[ev_stats_te, ev_upd], p=1))
                for e in range(NE):
                    evs1 = te(MM(P[0][0:1, :], onesb[:, 0:1], shifted(e, c0), e == 0, e == NE - 1),
                              w=([ev_row, ev_upd] if e == 0 else ()), p=(1 if e == NE - 1 else 0))
                for e in range(NE):
                    evs2 = te(MM(P[1][0:1, :], onesb[:, 0:1], sqb[:, e, :], e == 0, e == NE - 1),
                              w=[ev_sq[e]], p=(1 if e == NE - 1 else 0))
                ev_stats_te = evs2
                ve(TS1(statf[0:1, 0:512], P[0][0:1, :], cc_sb[0:1, 3:4], ALU.mult), w=[evs1], p=1)
                ev = vpad()
                ve(TT(statf[0:1, 1536:2048], statf[0:1, 0:512], statf[0:1, 0:512], ALU.mult), w=[ev], p=1)
                ev = vpad()
                ve(STT(statf[0:1, 512:1024], P[1][0:1, :], cc_sb[0:1, 3:4], statf[0:1, 1536:2048],
                       ALU.mult, ALU.subtract), w=[evs2, ev], p=1)
                ev = vpad()
                se(ACT(statf[0:1, 512:1024], statf[0:1, 512:1024], AF.Sqrt, bias=cc_sb[0:1, 1:2]), w=[ev], p=1)
                ev = spad()
                ve(RCP(statf[0:1, 512:1024], statf[0:1, 512:1024]), w=[ev], p=1)
                ev = vpad()
                ve(TT(statf[0:1, 1024:1536], statf[0:1, 0:512], statf[0:1, 512:1024], ALU.mult), w=[ev], p=1)
                ve(VCP(statb[0:1, 0:512], statf[0:1, 512:1024]), p=1)
                ev7 = vpad()
                ve(VCP(statb[0:1, 512:1024], statf[0:1, 1024:1536]), w=[ev7], p=1)
                ev8 = vpad()
                ev_row = ev8
                evh2 = None
                for e in range(NE):
                    lrow = lng_sb[0:1, (l * NE + e) * 128:(l * NE + e + 1) * 128]
                    evg1 = te(MM(P[2][:, :], lrow, statb[0:1, 0:512], True, True),
                              w=[ev7, ev_apply[0]], p=1)
                    evg2 = te(MM(P[3][:, :], lrow, statb[0:1, 512:1024], True, True),
                              w=[ev8, ev_apply[1]], p=1)
                    evt1 = ve(TT(h2T[:, cs, e, :], shifted(e, c0), P[2][:, :], ALU.mult),
                              w=[evg1, ev_h2last[0]], p=1)
                    ev_apply[0] = evt1
                    evh2 = ve(STT(h2T[:, cs, e, :], h2T[:, cs, e, :], lnb_sb[:, l, e:e + 1],
                                  P[3][:, :], ALU.add, ALU.subtract), w=[evg2], p=1)
                    ev_apply[1] = evh2
                if c > 0 and S0P >= 3:
                    evu = None
                    for ot in range(NE):
                        evu = ve(TT(yT[:, ot, c0 - 512:c0], yT[:, ot, c0 - 512:c0],
                                    delta2[:, ot, :], ALU.add),
                                 w=[ev_d2[ot], ev_stats_te, ev_sq[-1]], p=1)
                    ev_upd = evu
                if S0P < 2:
                    continue
                if c == 0:
                    for of in range(3):
                        load_w1(of)
                    nl1[0] = 3
                evma = None
                for p_ in range(NKT2):
                    ofa, ofg = 2 * p_, 2 * p_ + 1
                    while nl1[0] <= min(ofg + 2, NOF - 1):
                        load_w1(nl1[0])
                        nl1[0] += 1
                    for e in range(NE):
                        evma = te(MM(P[4][:, :], w1w[:, ofa % 4, e, :], h2T[:, cs, e, :],
                                     e == 0, e == NE - 1),
                                  w=([ev_dma1[ofa], evh2, ev_p4war[0]]
                                     if e == 0 else ()), p=(1 if e == NE - 1 else 0))
                    ev_w1slot[ofa % 4] = evma
                    for e in range(NE):
                        evmg = te(MM(P[5][:, :], w1w[:, ofg % 4, e, :], h2T[:, cs, e, :],
                                     e == 0, e == NE - 1),
                                  w=([ev_dma1[ofg], ev_p5war[0]] if e == 0 else ()),
                                  p=(1 if e == NE - 1 else 0))
                    ev_w1slot[ofg % 4] = evmg
                    evgel = se(ACT(gel[:, p_ % 2, :], P[5][:, :], AF.Gelu,
                                   bias=b1_sb[:, l, ofg:ofg + 1]), w=[evmg, ev_gelw[p_ % 2]], p=1)
                    ev_p5war[0] = evgel
                    ev_h3[p_] = ve(STT(h3[:, p_, :], P[4][:, :], b1_sb[:, l, ofa:ofa + 1],
                                       gel[:, p_ % 2, :], ALU.add, ALU.mult),
                                   w=[evgel, evma], p=1)
                    ev_p4war[0] = ev_h3[p_]
                    ev_gelw[p_ % 2] = ev_h3[p_]
                ev_h2last[0] = evmg
                nl1[0] = 0
                for i in range(NOF):
                    ev_dma1[i] = None
                if S0P < 3:
                    continue
                load_w2(0)
                for ot in range(NE):
                    if ot + 1 < NE:
                        load_w2(ot + 1)
                    bank = P[6 + ot % 2]
                    for kt in range(NKT2):
                        evm2 = te(MM(bank[:, :], w2w[:, ot % 2, kt, :], h3[:, kt, :],
                                     kt == 0, kt == NKT2 - 1),
                                  w=([ev_dma2[ot % 2], ev_d2[(ot - 2) % NE], ev_h3[kt]] if kt == 0
                                     else [ev_h3[kt]]),
                                  p=(1 if kt == NKT2 - 1 else 0))
                    ev_w2war[ot % 2] = evm2
                    ev_d2[ot] = se(ACT(delta2[:, ot, :], bank[:, :], AF.Identity,
                                       bias=b2_sb[:, l, ot:ot + 1]), w=[evm2, ev_upd], p=1)
                pump(2)
            if S0C == 8 and S0P >= 3:
                evu = None
                for ot in range(NE):
                    evu = ve(TT(yT[:, ot, PAD + 7 * 512:PAD + T], yT[:, ot, PAD + 7 * 512:PAD + T],
                                delta2[:, ot, :], ALU.add), w=[ev_d2[ot]], p=1)
            part2.dbg = (delta2, h3)
            em.barrier()

        wq_ev = ev_w0
        for l in range(DEPTH):
            oTh_dbg = part1(l, wq_ev)
            if stage == "P1" and l == 0:
                _dbg_dump(em, gp, DMA, dbg_h, yT)
                em.add("gp", lambda h, a=dbg_h, b=oTh_dbg: h.dma_start(
                    out=a[:, 2, 0:HEADS * WIN], in_=b[:, :, :].rearrange("p a b -> p (a b)")), pub=16)
                em.add("gp", lambda h, a=dbg_h, b=ptil_dbg[0]: h.dma_start(
                    out=a[:, 3, 0:8 * WIN], in_=b[:, :, :].rearrange("p a b -> p (a b)")), pub=16)
                em.add("gp", lambda h, a=dbg_h, b=ptil_dbg[1]: h.dma_start(
                    out=a[:, 4, 0:HEADS * WIN], in_=b[:, :, :].rearrange("p a b -> p (a b)")), pub=16)
                for hh_ in range(HEADS):
                    em.add("gp", lambda h, a=dbg_h, b=ptil_dbg[2], hh2=hh_: h.dma_start(
                        out=a[:, 5, hh2 * WIN:(hh2 + 1) * WIN], in_=b[:, hh2, 1, :]), pub=16)
                em.add("gp", lambda h, a=dbg_h, b=ptil_dbg[3]: h.dma_start(
                    out=a[:, 6, 0:WIN], in_=b[:, :]), pub=16)
                em.add("gp", lambda h, a=dbg_h, b=ptil_dbg[4]: h.dma_start(
                    out=a[:, 7, 0:WIN], in_=b[:, 0, :]), pub=16)
                em.add("gp", lambda h, a=dbg_h, b=ptil_dbg[5]: h.dma_start(
                    out=a[:, 0, 4104 - 512 - 8:4104 - 8], in_=b[:, :]), pub=16)
                em.add("gp", lambda h, a=dbg_h: h.dma_start(
                    out=a[:, 1, 0:1536], in_=wqkv_sb[:, 0, :]), pub=16)
                evlast = em.add("gp", lambda h, a=dbg_h: h.dma_start(
                    out=a[:, 1, 1536:3072], in_=wqkv_sb[:, 4, :]), pub=16)
                em.add("gp", lambda h: h.nop(), waits=[evlast])
                em.barrier()
                em.check()
                _emit(em, sems, blk)
                return nc
            if stage == "P11" and l == 1:
                _dbg_dump(em, gp, DMA, dbg_h, yT)
                _emit(em, sems, blk)
                return nc
            if l + 1 < DEPTH:
                load_wqkv_wo(l + 1)
                pump(NE + HEADS)
                wq_ev = last_prep[0]
                queue_tail(l + 1)
            part2(l)
            if stage == "S1" and l == 1:
                _dbg_dump(em, gp, DMA, dbg_h, yT)
                _emit(em, sems, blk)
                return nc
            if stage == "S0" and l == 0:
                _dbg_dump(em, gp, DMA, dbg_h, yT)
                if os.environ.get("S0DBG", "0") == "0":
                    _emit(em, sems, blk)
                    return nc
                d2_, h3_ = part2.dbg
                em.add("gp", lambda h, a=dbg_h: h.dma_start(
                    out=a[:, 0, 0:NKT2 * 128], in_=w2w[:, 1, :, :].rearrange("p k c -> p (k c)")), pub=16)
                em.add("gp", lambda h, a=dbg_h: h.dma_start(
                    out=a[:, 1, 0:NKT2 * 128], in_=w2w[:, 0, :, :].rearrange("p k c -> p (k c)")), pub=16)
                em.add("gp", lambda h, a=dbg_h, b=d2_: h.dma_start(
                    out=a[:, 2, 0:512], in_=b[:, 7, :]), pub=16)
                em.add("gp", lambda h, a=dbg_h, b=h3_: h.dma_start(
                    out=a[:, 3, 0:512], in_=b[:, 21, :]), pub=16)
                evl_ = em.add("gp", lambda h, a=dbg_h, b=d2_: h.dma_start(
                    out=a[:, 4, 0:512], in_=b[:, 6, :]), pub=16)
                em.add("gp", lambda h: h.nop(), waits=[evl_])
                em.barrier()
                em.check()
                _emit(em, sems, blk)
                return nc

        Fz = Ar(BASE)
        xf1 = Fz.get([128, 2, 1024], F32)
        ytk = Fz.get([128, 2, 1024], BF16)
        sumf = Fz.get([128, 2, 1024], F32)
        sqf = Fz.get([128, 1024], F32)
        ocp = Fz.get([128, 2, 1024], F32)
        statg = Fz.get([128, 16], F32)

        ev_xf1 = [None, None]
        ev_sumf = [None, None]
        ev_ytk = [None, None]
        ev_out = [None, None]
        ev_og = [None, None]
        ev_sqf = None
        for tt in range(NTT):
            s = tt % 2
            ev_xf1[s] = gp(DMA(xf1[:, s, :], x1_h[tt]), w=[ev_sumf[s], ev_xf1[s]], p=16)
            for e in range(NE):
                evt = te(TRB(Pb[7][0:128, e * 128:(e + 1) * 128],
                             yT[:, e, PAD + tt * 128:PAD + (tt + 1) * 128]),
                         w=([ev_ytk[1 - s]] if e == 0 else ()), p=(1 if e == NE - 1 else 0))
            ev_ytk[s] = ve(VCP(ytk[:, s, :].rearrange("p (e c) -> p e c", e=8),
                               Pb[7][:, 0:1024].rearrange("p (e c) -> p e c", e=8)),
                           w=[evt], p=1)
            ev_sumf[s] = ve(TT(sumf[:, s, :], xf1[:, s, :], ytk[:, s, :], ALU.add),
                            w=[ev_xf1[s], ev_ytk[s]], p=1)
            sm = sumf[:, s, :]
            ve(RED(statg[:, 0:1], sm, ALU.add), p=1)
            ev = vpad()
            ve(TS1(statg[:, 1:2], statg[:, 0:1], cc_sb[:, 3:4], ALU.mult), w=[ev], p=1)
            ev1 = vpad()
            evsq = se(ACT(sqf[:, :], sm, AF.Square), w=[ev_sumf[s], ev_sqf], p=1)
            ve(RED(statg[:, 2:3], sqf[:, :], ALU.add), w=[evsq], p=1)
            ev_sqf = vpad()
            ve(TT(statg[:, 3:4], statg[:, 1:2], statg[:, 1:2], ALU.mult), w=[ev1], p=1)
            ev = vpad()
            ve(STT(statg[:, 4:5], statg[:, 2:3], cc_sb[:, 3:4], statg[:, 3:4],
                   ALU.mult, ALU.subtract), w=[ev, ev_sqf], p=1)
            ev = vpad()
            se(ACT(statg[:, 5:6], statg[:, 4:5], AF.Sqrt, bias=cc_sb[:, 1:2]), w=[ev], p=1)
            ev = spad()
            ve(RCP(statg[:, 6:7], statg[:, 5:6]), w=[ev], p=1)
            ev = vpad()
            ev_out[s] = ve(TS(ocp[:, s, :], sm, statg[:, 1:2], statg[:, 6:7],
                              ALU.subtract, ALU.mult), w=[ev_og[s], ev], p=1)
            ev_og[s] = gp(DMA(out_h[tt], ocp[:, s, :]), w=[ev_out[s]], p=16)
        em.add("gp", lambda h: h.nop(), waits=[ev_og[0], ev_og[1]])
        em.barrier()
        em.check()
        _emit(em, sems, blk)
    return nc


def _dbg_dump(em, gp, DMA, dbg_h, yT):
    em.barrier()
    ev = em.add("gp", lambda h, a=dbg_h, b=yT: h.dma_start(out=a, in_=b[:, :, :]), pub=16)
    em.add("gp", lambda h: h.nop(), waits=[ev])
    em.barrier()
    em.check()


def _emit(em, sems, blk):
    @blk.tensor
    def _(h):
        for f in em.ops["te"]:
            f(h, sems)

    @blk.vector
    def _(h):
        for f in em.ops["ve"]:
            f(h, sems)

    @blk.scalar
    def _(h):
        for f in em.ops["se"]:
            f(h, sems)

    @blk.sync
    def _(h):
        for f in em.ops["sy"]:
            f(h, sems)

    @blk.gpsimd
    def _(h):
        for f in em.ops["gp"]:
            f(h, sems)


def _in_maps(inputs, stage="FULL"):
    shared = _prep_shared(inputs)
    x = np.asarray(inputs["L0_attn_x"], np.float32)
    maps = []
    for b in range(8):
        m = dict(shared)
        m["x"] = np.ascontiguousarray(x[b].reshape(NTT, 128, 1024))
        maps.append(m)
    return maps


def kernel(**inputs):
    nc = build_program("FULL")
    maps = _in_maps(inputs)
    res = run_bass_kernel_spmd(nc, maps, core_ids=list(range(8)))
    out = np.stack([res.results[b]["out"].reshape(T, 1024) for b in range(8)])
    return out
